# version 03
# baseline (speedup 1.0000x reference)
import numpy as np
import ml_dtypes
import concourse.bass as bass
import concourse.mybir as mybir
from concourse.bass_utils import run_bass_kernel_spmd

F32 = mybir.dt.float32
BF16 = mybir.dt.bfloat16
AF = mybir.ActivationFunctionType
ALU = mybir.AluOpType

NCORES = 8
D = 1024
T = 2048
TH = 1024
NST = 16
DTR = 64
DCH = 8
HID = 4096
MH = HID // 128
SLAB = 512
NSL = T // SLAB
BIG = 1.0e30
DT_MIN, DT_MAX = 0.001, 0.1


class Prog:
    def __init__(self, nc):
        self.nc = nc
        self.events = []
        self.cnt = {}
        self.sems = {}
        self.guards = []

    def mksem(self, name):
        g = self.nc.semaphore(name)
        h = g.__enter__()
        self.guards.append(g)
        self.sems[name] = h
        self.cnt[name] = 0

    def step(self, eng, emit, deps=(), inc=1, sem=None):
        sname = sem or eng
        waits = [(self.sems[s], v) for (s, v) in deps if v > 0]
        if inc:
            self.cnt[sname] += inc
        after = self.cnt[sname]
        smh = self.sems[sname]

        def fn(e, waits=waits, inc=inc, smh=smh, emit=emit):
            for h, v in waits:
                e.wait_ge(h, v)
            inst = emit(e)
            if inc:
                inst.then_inc(smh, inc)

        self.events.append((eng, fn))
        return (sname, after)

    def dma(self, eng, transfers, deps=(), sem=None):
        if sem is None:
            pool = self.dma_pools[eng]
            sname = pool[self.dma_rr[eng] % len(pool)]
            self.dma_rr[eng] += 1
        else:
            sname = sem
        waits = [(self.sems[s], v) for (s, v) in deps if v > 0]
        prev_total = self.cnt[sname]
        self.cnt[sname] += 16 * len(transfers)
        after = self.cnt[sname]
        smh = self.sems[sname]

        def fn(e, waits=waits, transfers=transfers, smh=smh, prev=prev_total):
            if prev > 0:
                e.wait_ge(smh, prev)
            for h, v in waits:
                e.wait_ge(h, v)
            for out, in_ in transfers:
                e.dma_start(out=out, in_=in_).then_inc(smh, 16)

        self.events.append((eng, fn))
        return (sname, after)

    def wait_only(self, eng, deps):
        waits = [(self.sems[s], v) for (s, v) in deps if v > 0]

        def fn(e, waits=waits):
            for h, v in waits:
                e.wait_ge(h, v)

        self.events.append((eng, fn))

    def emit(self, block):
        evs = self.events

        @block.sync
        def _(e):
            for k, fn in evs:
                if k == "sp":
                    fn(e)

        @block.tensor
        def _(e):
            for k, fn in evs:
                if k == "pe":
                    fn(e)

        @block.scalar
        def _(e):
            for k, fn in evs:
                if k == "act":
                    fn(e)

        @block.vector
        def _(e):
            for k, fn in evs:
                if k == "dve":
                    fn(e)

        @block.gpsimd
        def _(e):
            for k, fn in evs:
                if k == "pool":
                    fn(e)


def build_program(stage=4):
    full = stage >= 4
    nc = bass.Bass("TRN2", target_bir_lowering=False, detect_race_conditions=False)

    dram = {}

    def din(name, shape, dt=F32):
        dram[name] = nc.dram_tensor(name, list(shape), dt, kind="ExternalInput").ap()
        return dram[name]

    x_h = din("x", [D, T])
    w_in_h = din("w_in", [D, D])
    w_x_h = din("w_x", [D, 96])
    w_dt_h = din("w_dt", [DTR, D])
    bdt_h = din("b_dt", [128, DCH])
    Alog_h = din("A_log", [D, NST])
    Dp_h = din("Dp", [128, DCH])
    w1m_h = din("w1m", [32, 128, 8, 128], BF16)
    w2m_h = din("w2m", [32, 128, 8, 128], BF16)
    h1m_h = din("h1m", [32, 128, 32, 128], BF16)
    h2m_h = din("h2m", [32, 128, 32, 128], BF16)
    w3m_h = din("w3m", [8, 128, 32, 128], BF16)
    s1_h = din("s1", [128, 32])
    s2_h = din("s2", [128, 32])
    sh1_h = din("sh1", [128, 32])
    sh2_h = din("sh2", [128, 32])
    s3_h = din("s3", [128, DCH])
    m2b_h = din("m2b", [128, 72])
    ones_h = din("onesc", [128, 1])
    cns_h = din("cns", [128, 4])

    out_h = nc.dram_tensor("out", [D, T], F32, kind="ExternalOutput").ap()
    dbg = {}
    if stage == 5:
        dbg["dbg_xmid"] = nc.dram_tensor("dbg_xmid", [D, T], F32, kind="ExternalOutput").ap()
    if stage < 4:
        def dout(name, shape, dt=F32):
            dbg[name] = nc.dram_tensor(name, list(shape), dt, kind="ExternalOutput").ap()
            return dbg[name]
        dout("dbg_dt", [128, DCH * T], BF16)
        dout("dbg_u", [128, NSL * DCH * SLAB], BF16)
        dout("dbg_g", [NST, T])
        dout("dbg_c", [NST, T])
        dout("dbg_xn", [128, DCH * SLAB], BF16)
        dout("dbg_rows", [3, SLAB])
        dout("dbg_ab", [128, 2 * SLAB])
        if stage >= 2:
            dout("dbg_hb", [128, 128])
            dout("dbg_hin", [128, 128])
            dout("dbg_grep", [128, TH])
            dout("dbg_bar", [128, TH])
            dout("dbg_H", [128, TH])
        if stage >= 3:
            dout("dbg_xmid", [D, T])

    g_dram = nc.dram_tensor("g_scr", [NST, T], F32).ap()
    c_dram = nc.dram_tensor("c_scr", [NST, T], F32).ap()
    snd_d = nc.dram_tensor("snd_scr", [128, 136], F32)
    rcv_d = nc.dram_tensor("rcv_scr", [128 * NCORES, 136], F32)
    xmid_d = nc.dram_tensor("xmid_scr", [D, T], F32).ap()
    lnrow_d = nc.dram_tensor("lnrow_scr", [2, SLAB], F32).ap()

    p = Prog(nc)
    for s in ["pe", "act", "dve", "pool", "sp", "cc"]:
        p.mksem(s)
    p.dma_pools = {"sp": [], "pool": []}
    p.dma_rr = {"sp": 0, "pool": 0}
    for i in range(6):
        n = "spq%d" % i
        p.mksem(n)
        p.dma_pools["sp"].append(n)
    for i in range(8):
        n = "plq%d" % i
        p.mksem(n)
        p.dma_pools["pool"].append(n)

    live = {}

    def alloc(name, shape, dt, side="left"):
        g = nc.sbuf_tensor(name, list(shape), dt, side=side)
        t = g.__enter__()
        live[name] = g
        return t

    def free(*names):
        for n in names:
            live.pop(n).__exit__(None, None, None)

    palloc_guards = {}

    def palloc(name, shape, dt=F32):
        g = nc.psum_tensor(name, list(shape), dt)
        t = g.__enter__()
        palloc_guards[name] = g
        return t

    def pfree(*names):
        for n in names:
            palloc_guards.pop(n).__exit__(None, None, None)

    A_sc = alloc("A_sc", [128, 128], F32)
    A_bf = alloc("A_bf", [128, 128], BF16)
    bdt_sc = alloc("bdt_sc", [128, DCH], F32)
    Dp_sc = alloc("Dp_sc", [128, DCH], F32)
    s1_sc = alloc("s1_sc", [128, 32], F32)
    s2_sc = alloc("s2_sc", [128, 32], F32)
    sh1_sc = alloc("sh1_sc", [128, 32], F32)
    sh2_sc = alloc("sh2_sc", [128, 32], F32)
    s3_sc = alloc("s3_sc", [128, DCH], F32)
    m2b_sb = alloc("m2b_sb", [128, 72], F32)
    ones_f = alloc("ones_f", [128, 1], F32)
    ones_b = alloc("ones_b", [128, 1], BF16)
    cns = alloc("cns_sb", [128, 4], F32)
    ones8 = alloc("ones8", [128, 8], F32)

    wcur = [alloc("wcur0", [128, 4096], BF16, side="right"),
            alloc("wcur1", [128, 4096], BF16, side="right")]

    Y = alloc("Y_all", [128, DCH * T], F32)
    dt_all = alloc("dt_all", [128, DCH * T], BF16)

    e_cl = p.dma("sp", [
        (bdt_sc[:], bdt_h[:]), (Dp_sc[:], Dp_h[:]),
        (s1_sc[:], s1_h[:]), (s2_sc[:], s2_h[:]),
        (sh1_sc[:], sh1_h[:]), (sh2_sc[:], sh2_h[:]), (s3_sc[:], s3_h[:]),
        (m2b_sb[:], m2b_h[:]), (ones_f[:], ones_h[:]), (cns[:], cns_h[:]),
    ] + [(A_sc[:, dc * 16:(dc + 1) * 16], Alog_h[dc * 128:(dc + 1) * 128, :])
         for dc in range(DCH)])
    e_A1 = p.step("act", lambda e: e.activation(A_sc[:], A_sc[:], AF.Exp), deps=[e_cl])
    e_A2 = p.step("dve", lambda e: e.tensor_scalar_mul(A_sc[:], A_sc[:], -1.0), deps=[e_A1])
    e_A3 = p.step("dve", lambda e: e.tensor_copy(A_bf[:], A_sc[:]), deps=[e_A2])
    e_ones = p.step("dve", lambda e: e.tensor_copy(ones_b[:], ones_f[:]), deps=[e_cl])
    e_ones8 = p.step("dve", lambda e: e.memset(ones8[:], 1.0), deps=[])

    w_in_bf = alloc("w_in_bf", [128, DCH * D], BF16)
    wx_bf = alloc("wx_bf", [128, DCH * 96], BF16)
    wdt_bf = alloc("wdt_bf", [DTR, D], BF16)
    xb = alloc("xb", [128, DCH * SLAB], BF16)
    sq = alloc("sq", [128, DCH * SLAB], BF16)
    xin_bf = alloc("xin_bf", [128, DCH * SLAB], BF16)
    mu_r = alloc("mu_r", [1, SLAB], F32)
    vr_r = alloc("vr_r", [1, SLAB], F32)
    bw_r = alloc("bw_r", [1, SLAB], F32)
    arep = alloc("arep", [128, SLAB], F32)
    brep = alloc("brep", [128, SLAB], F32)
    dtl_bf = alloc("dtl_bf", [DTR, SLAB], BF16)
    B_sb = alloc("B_sb", [NST, SLAB], F32)
    g_sb = alloc("g_sb", [NST, SLAB], F32)
    z_t = alloc("z_t", [128, SLAB], F32)
    r_t = alloc("r_t", [128, SLAB], F32)
    wld = alloc("wld", [128, D], F32)

    ev_wcast = []
    prev_cast = (None, 0)
    for k in range(DCH):
        dep = [] if k == 0 else [prev_cast]
        e_l = p.dma("sp", [(wld[:], w_in_h[k * 128:(k + 1) * 128, :])], deps=dep)
        prev_cast = p.step("dve", lambda e, k=k: e.tensor_copy(
            w_in_bf[:, k * D:(k + 1) * D], wld[:]), deps=[e_l])
        ev_wcast.append(prev_cast)
    for k in range(DCH):
        e_l = p.dma("sp", [(wld[:, 0:96], w_x_h[k * 128:(k + 1) * 128, :])], deps=[prev_cast])
        prev_cast = p.step("dve", lambda e, k=k: e.tensor_copy(
            wx_bf[:, k * 96:(k + 1) * 96], wld[:, 0:96]), deps=[e_l])
    e_l = p.dma("sp", [(wld[0:DTR, :], w_dt_h[:])], deps=[prev_cast])
    e_wdt = p.step("dve", lambda e: e.tensor_copy(wdt_bf[:], wld[0:DTR, :]), deps=[e_l])
    ev_wready = e_wdt

    ps_su = palloc("ps_su", [1, SLAB])
    ps_sq = palloc("ps_sq", [1, SLAB])
    ps_a = palloc("ps_a", [128, SLAB])
    ps_b = palloc("ps_b", [128, SLAB])
    ps_dl = palloc("ps_dl", [DTR, SLAB])
    ps_B = palloc("ps_B", [NST, SLAB])
    ps_C = palloc("ps_C", [NST, SLAB])
    ps_g = palloc("ps_g", [NST, SLAB])

    e_dt_done = []
    e_g_done = []
    e_prev_xdma = (None, 0)
    last_xin_evac = [(None, 0), (None, 0)]
    fence = []

    for s in range(NSL):
        c0 = s * SLAB

        def ycols(dc, off=c0, w=SLAB):
            return Y[:, dc * T + off: dc * T + off + w]

        e_x = p.dma("pool", [(ycols(dc), x_h[dc * 128:(dc + 1) * 128, c0:c0 + SLAB])
                             for dc in range(DCH)], deps=[e_prev_xdma])
        e_prev_xdma = e_x
        e_cast = p.step("dve", lambda e, c0=c0: [
            e.tensor_copy(xb[:, dc * SLAB:(dc + 1) * SLAB],
                          Y[:, dc * T + c0: dc * T + c0 + SLAB])
            for dc in range(DCH)][-1], deps=[e_x, e_ones] + fence)
        e_sqr = p.step("act", lambda e, c0=c0: [
            e.activation(sq[:, dc * SLAB:(dc + 1) * SLAB],
                         Y[:, dc * T + c0: dc * T + c0 + SLAB], AF.Square)
            for dc in range(DCH)][-1], deps=[e_x, e_A1] + fence)
        e_st1 = p.step("pe", lambda e: [
            e.matmul(ps_su[:], ones_b[:, 0:1], xb[:, dc * SLAB:(dc + 1) * SLAB],
                     start=(dc == 0), stop=(dc == DCH - 1))
            for dc in range(DCH)][-1], deps=[e_cast])
        e_st2 = p.step("pe", lambda e: [
            e.matmul(ps_sq[:], ones_b[:, 0:1], sq[:, dc * SLAB:(dc + 1) * SLAB],
                     start=(dc == 0), stop=(dc == DCH - 1))
            for dc in range(DCH)][-1], deps=[e_sqr])
        e_mu = p.step("dve", lambda e: e.tensor_scalar_mul(mu_r[:], ps_su[:], 1.0 / D),
                      deps=[e_st1])
        e_vr = p.step("dve", lambda e: (
            e.tensor_mul(vr_r[:], mu_r[:], mu_r[:]),
            e.scalar_tensor_tensor(vr_r[:], ps_sq[:], 1.0 / D, vr_r[:],
                                   ALU.mult, ALU.subtract))[-1],
            deps=[e_mu, e_st2])
        e_rs = p.step("act", lambda e: (
            e.activation(vr_r[:], vr_r[:], AF.Ln, bias=cns[0:1, 0:1]),
            e.activation(vr_r[:], vr_r[:], AF.Exp, scale=cns[0:1, 1:2]))[-1], deps=[e_vr])
        e_bw = p.step("dve", lambda e: e.scalar_tensor_tensor(
            bw_r[:], mu_r[:], -1.0, vr_r[:], ALU.mult, ALU.mult), deps=[e_rs])
        e_ro = p.dma("pool", [(lnrow_d[0:1, :], vr_r[:]), (lnrow_d[1:2, :], bw_r[:])],
                     deps=[e_bw])
        e_bc = p.dma("pool", [(arep[:], lnrow_d[0:1, :].broadcast_to([128, SLAB])),
                              (brep[:], lnrow_d[1:2, :].broadcast_to([128, SLAB]))],
                     deps=[e_ro])
        e_xn = p.step("dve", lambda e, c0=c0: [
            (e.tensor_mul(z_t[:], Y[:, dc * T + c0: dc * T + c0 + SLAB], arep[:]),
             e.tensor_add(xb[:, dc * SLAB:(dc + 1) * SLAB], z_t[:], brep[:]))
            for dc in range(DCH)][-1][-1], deps=[e_bc, e_st1, e_st2])
        e_evac_all = []
        for m in range(DCH):
            ps = ps_a if m % 2 == 0 else ps_b
            dep_mm = [e_xn, ev_wcast[-1], last_xin_evac[m % 2]]
            e_mm = p.step("pe", lambda e, m=m, ps=ps: [
                e.matmul(ps[:], w_in_bf[:, k * D + m * 128:k * D + (m + 1) * 128],
                         xb[:, k * SLAB:(k + 1) * SLAB],
                         start=(k == 0), stop=(k == DCH - 1))
                for k in range(DCH)][-1], deps=dep_mm)
            e_ev = p.step("dve", lambda e, m=m, ps=ps: e.tensor_copy(
                xin_bf[:, m * SLAB:(m + 1) * SLAB], ps[:]), deps=[e_mm])
            e_y = p.step("dve", lambda e, m=m, c0=c0: e.scalar_tensor_tensor(
                Y[:, m * T + c0: m * T + c0 + SLAB],
                xin_bf[:, m * SLAB:(m + 1) * SLAB], Dp_sc[:, m:m + 1],
                Y[:, m * T + c0: m * T + c0 + SLAB], ALU.mult, ALU.add),
                deps=[e_ev])
            last_xin_evac[m % 2] = e_y
            e_evac_all.append(e_ev)
        e_pdl = p.step("pe", lambda e: [
            e.matmul(ps_dl[:], wx_bf[:, k * 96:k * 96 + DTR],
                     xin_bf[:, k * SLAB:(k + 1) * SLAB],
                     start=(k == 0), stop=(k == DCH - 1))
            for k in range(DCH)][-1], deps=e_evac_all + [ev_wready])
        e_pB = p.step("pe", lambda e: [
            e.matmul(ps_B[:], wx_bf[:, k * 96 + DTR:k * 96 + DTR + NST],
                     xin_bf[:, k * SLAB:(k + 1) * SLAB],
                     start=(k == 0), stop=(k == DCH - 1))
            for k in range(DCH)][-1], deps=[e_pdl])
        e_pC = p.step("pe", lambda e: [
            e.matmul(ps_C[:], wx_bf[:, k * 96 + 80:k * 96 + 96],
                     xin_bf[:, k * SLAB:(k + 1) * SLAB],
                     start=(k == 0), stop=(k == DCH - 1))
            for k in range(DCH)][-1], deps=[e_pB])
        e_dlb = p.step("dve", lambda e: e.tensor_copy(dtl_bf[:], ps_dl[:]), deps=[e_pdl])
        e_Bsb = p.step("dve", lambda e: e.tensor_copy(B_sb[:], ps_B[:]), deps=[e_pB])
        e_Csb = p.step("dve", lambda e: e.tensor_copy(g_sb[:], ps_C[:]), deps=[e_pC])
        e_cdma = p.dma("pool", [(c_dram[:, c0:c0 + SLAB], g_sb[:])], deps=[e_Csb])
        e_dt_slab = []
        prev_sp2 = (None, 0)
        for m in range(DCH):
            ps = ps_a if m % 2 == 0 else ps_b
            e_mm = p.step("pe", lambda e, m=m, ps=ps: e.matmul(
                ps[:], wdt_bf[:, m * 128:(m + 1) * 128], dtl_bf[:],
                start=True, stop=True), deps=[e_dlb, last_xin_evac[m % 2]])
            e_sp1 = p.step("act", lambda e, m=m, ps=ps: (
                e.activation(r_t[:], ps[:], AF.Relu, bias=bdt_sc[:, m:m + 1]),
                e.activation(z_t[:], ps[:], AF.Abs, bias=bdt_sc[:, m:m + 1]),
                e.activation(z_t[:], z_t[:], AF.Exp, scale=cns[:, 2:3]),
                e.activation(z_t[:], z_t[:], AF.Ln, bias=cns[:, 3:4]))[-1],
                deps=[e_mm, e_rs, prev_sp2])
            e_sp2 = p.step("dve", lambda e, m=m, c0=c0: (
                e.tensor_add(r_t[:], r_t[:], z_t[:]),
                e.tensor_scalar(dt_all[:, m * T + c0: m * T + c0 + SLAB],
                                r_t[:], DT_MAX - DT_MIN, DT_MIN, ALU.mult, ALU.add),
                e.tensor_mul(sq[:, m * SLAB:(m + 1) * SLAB],
                             dt_all[:, m * T + c0: m * T + c0 + SLAB],
                             xin_bf[:, m * SLAB:(m + 1) * SLAB]))[-1],
                deps=[e_sp1])
            last_xin_evac[m % 2] = e_sp2
            prev_sp2 = e_sp2
            e_dt_slab.append(e_sp2)
        e_dt_done.append(e_dt_slab[-1])
        e_wg = p.step("pe", lambda e: [
            e.matmul(ps_g[:], A_bf[:, k * 16:(k + 1) * 16],
                     sq[:, k * SLAB:(k + 1) * SLAB],
                     start=(k == 0), stop=(k == DCH - 1))
            for k in range(DCH)][-1], deps=e_dt_slab + [e_A3])
        e_g = p.step("dve", lambda e: e.tensor_mul(g_sb[:], B_sb[:], ps_g[:]),
                     deps=[e_wg, e_cdma, e_Bsb])
        e_gd = p.dma("pool", [(g_dram[:, c0:c0 + SLAB], g_sb[:])], deps=[e_g])
        e_g_done.append(e_gd)
        if stage < 4 and s == NSL - 1:
            p.dma("pool", [(dbg["dbg_xn"][:], xb[:])], deps=[e_xn, e_wg])
            p.dma("pool", [(dbg["dbg_rows"][0:1, :], mu_r[:]),
                           (dbg["dbg_rows"][1:2, :], vr_r[:]),
                           (dbg["dbg_rows"][2:3, :], bw_r[:])], deps=[e_bw, e_wg])
            p.dma("pool", [(dbg["dbg_ab"][:, 0:SLAB], arep[:]),
                           (dbg["dbg_ab"][:, SLAB:2 * SLAB], brep[:])],
                  deps=[e_xn, e_wg])
        fence = [e_wg, e_g, e_sp1, e_gd]

    e_p1 = e_g_done[-1]

    if stage < 4:
        p.dma("pool", [(dbg["dbg_dt"][:], dt_all[:])], deps=[e_dt_done[-1]] + e_dt_done)
        p.dma("pool", [(dbg["dbg_g"][:], g_dram[:]), (dbg["dbg_c"][:], c_dram[:])],
              deps=e_g_done)
        p.dma("pool", [(dbg["dbg_u"][:, 3 * DCH * SLAB:4 * DCH * SLAB], xin_bf[:])],
              deps=[e_p1])

    if stage >= 2:
        pfree("ps_g", "ps_C", "ps_B", "ps_dl", "ps_b", "ps_a", "ps_sq", "ps_su")
        free("wld", "r_t", "z_t", "g_sb", "B_sb", "dtl_bf", "brep", "arep",
             "bw_r", "vr_r", "mu_r", "xin_bf", "sq", "xb", "wdt_bf", "wx_bf",
             "w_in_bf")

        barA = [alloc("barA0", [128, TH], F32), alloc("barA1", [128, TH], F32)]
        Htr = [alloc("Htr0", [128, TH], F32), alloc("Htr1", [128, TH], F32)]
        g_rep = alloc("g_rep", [128, TH], F32)
        c_rep = alloc("c_rep", [128, TH], F32)
        tmp_t = alloc("tmp_t", [128, TH], F32)
        Hb = alloc("Hb", [128, 128], F32)
        Sdt_sb = alloc("Sdt_sb", [128, DCH], F32)
        hin = alloc("hin", [128, 128], F32)
        rcv_sb = alloc("rcv_sb", [128, 8 * 136], F32)
        ss_all = alloc("ss_all", [128, 64], F32)
        E_t = alloc("E_t", [128, 16], F32)

        dt_ready = e_dt_done[-1]
        g_ready = e_g_done[-1]

        e_scan = [(None, 0), (None, 0)]
        e_cp_u = {}
        e_cp_par = [(None, 0), (None, 0)]
        e_grep = (None, 0)
        ui = 0
        for h in range(2):
            for n in range(NST):
                e_grep = p.dma("pool", [
                    (g_rep[:], g_dram[n:n + 1, h * TH:(h + 1) * TH]
                     .broadcast_to([128, TH]))],
                    deps=[g_ready, e_scan[0], e_scan[1]])
                for dc in range(DCH):
                    u = dc * 16 + n
                    par = ui % 2
                    e_bar = p.step("act", lambda e, dc=dc, h=h, u=u, par=par: e.activation(
                        barA[par][:], dt_all[:, dc * T + h * TH: dc * T + (h + 1) * TH],
                        AF.Exp, scale=A_sc[:, u:u + 1]),
                        deps=[dt_ready, e_scan[par]])
                    init = 0.0 if h == 0 else Hb[:, u:u + 1]
                    dep_sc = [e_bar, e_grep, e_cp_par[par]]
                    if h == 1:
                        dep_sc.append(e_cp_u[u])
                    e_sc = p.step("dve", lambda e, par=par, init=init:
                                  e.tensor_tensor_scan(Htr[par][:], barA[par][:],
                                                       g_rep[:], init,
                                                       ALU.mult, ALU.add),
                                  deps=dep_sc)
                    e_cp_u[u] = p.step("pool", lambda e, par=par, u=u: e.tensor_copy(
                        Hb[:, u:u + 1], Htr[par][:, TH - 1:TH]), deps=[e_sc])
                    e_cp_par[par] = e_cp_u[u]
                    e_scan[par] = e_sc
                    ui += 1
        if stage < 4:
            p.dma("pool", [(dbg["dbg_grep"][:], g_rep[:]),
                           (dbg["dbg_bar"][:], barA[1][:]),
                           (dbg["dbg_H"][:], Htr[1][:])],
                  deps=[e_scan[0], e_scan[1]])
        e_sdt = p.step("dve", lambda e: [
            e.tensor_reduce(Sdt_sb[:, dc:dc + 1], dt_all[:, dc * T:(dc + 1) * T],
                            mybir.AxisListType.X, ALU.add)
            for dc in range(DCH)][-1], deps=[dt_ready, e_scan[0], e_scan[1]])
        e_snd = p.dma("pool", [(snd_d.ap()[:, 0:128], Hb[:]),
                               (snd_d.ap()[:, 128:136], Sdt_sb[:])],
                      deps=[e_sdt, e_scan[0], e_scan[1]])
        e_cc = p.step("pool", lambda e: e.collective_compute(
            "AllGather", ALU.bypass, replica_groups=[list(range(NCORES))],
            ins=[snd_d.ap().opt()], outs=[rcv_d.ap().opt()]),
            deps=[e_snd], sem="cc")
        e_rcv = p.dma("pool", [(rcv_sb[:, j * 136:(j + 1) * 136],
                                rcv_d.ap()[j * 128:(j + 1) * 128, :])
                               for j in range(NCORES)], deps=[e_cc])
        e_hin0 = p.step("dve", lambda e: e.memset(hin[:], 0.0), deps=[])
        e_ss = []
        for j in range(NCORES):
            e_c = p.step("dve", lambda e, j=j: e.tensor_scalar(
                ss_all[:, j * 8:(j + 1) * 8], ones8[:],
                m2b_sb[:, j * 9 + 8:j * 9 + 9], None, ALU.mult),
                deps=[e_rcv, e_ones8, e_cl])
            for i in range(NCORES):
                e_c = p.step("dve", lambda e, j=j, i=i: e.scalar_tensor_tensor(
                    ss_all[:, j * 8:(j + 1) * 8],
                    rcv_sb[:, i * 136 + 128:i * 136 + 136],
                    m2b_sb[:, j * 9 + i:j * 9 + i + 1],
                    ss_all[:, j * 8:(j + 1) * 8], ALU.mult, ALU.add),
                    deps=[e_c])
            e_ss.append(e_c)
        e_comb = e_hin0
        for j in range(NCORES):
            for dc in range(DCH):
                e_E = p.step("act", lambda e, j=j, dc=dc: e.activation(
                    E_t[:], A_sc[:, dc * 16:(dc + 1) * 16], AF.Exp,
                    scale=ss_all[:, j * 8 + dc:j * 8 + dc + 1]),
                    deps=[e_ss[j], e_comb])
                e_mu2 = p.step("dve", lambda e, j=j, dc=dc: e.tensor_mul(
                    E_t[:], E_t[:],
                    rcv_sb[:, j * 136 + dc * 16:j * 136 + (dc + 1) * 16]),
                    deps=[e_E, e_hin0])
                e_comb = p.step("dve", lambda e, dc=dc: e.tensor_add(
                    hin[:, dc * 16:(dc + 1) * 16],
                    hin[:, dc * 16:(dc + 1) * 16], E_t[:]),
                    deps=[e_mu2])
        e_hin = e_comb

        if stage < 4:
            p.dma("pool", [(dbg["dbg_hb"][:], Hb[:])], deps=[e_snd])
            p.dma("pool", [(dbg["dbg_hin"][:], hin[:])], deps=[e_hin])

    if stage >= 3:
        e_scanB = [(None, 0), (None, 0)]
        e_cp_u = {}
        e_cp_par = [(None, 0), (None, 0)]
        e_ymul = [(None, 0), (None, 0)]
        e_grep = (None, 0)
        e_crep = (None, 0)
        ui = 0
        for h in range(2):
            for n in range(NST):
                dep_rep = [e_scanB[0], e_scanB[1], e_ymul[0], e_ymul[1], e_hin]
                e_grep = p.dma("pool", [
                    (g_rep[:], g_dram[n:n + 1, h * TH:(h + 1) * TH]
                     .broadcast_to([128, TH]))], deps=dep_rep)
                e_crep = p.dma("pool", [
                    (c_rep[:], c_dram[n:n + 1, h * TH:(h + 1) * TH]
                     .broadcast_to([128, TH]))], deps=dep_rep)
                for dc in range(DCH):
                    u = dc * 16 + n
                    par = ui % 2
                    e_bar = p.step("act", lambda e, dc=dc, h=h, u=u, par=par: e.activation(
                        barA[par][:], dt_all[:, dc * T + h * TH: dc * T + (h + 1) * TH],
                        AF.Exp, scale=A_sc[:, u:u + 1]),
                        deps=[e_scanB[par], e_hin])
                    init = hin[:, u:u + 1] if h == 0 else Hb[:, u:u + 1]
                    dep_sc = [e_bar, e_grep, e_crep, e_cp_par[par], e_ymul[par]]
                    if h == 1:
                        dep_sc.append(e_cp_u[u])
                    e_sc = p.step("dve", lambda e, par=par, init=init:
                                  e.tensor_tensor_scan(Htr[par][:], barA[par][:],
                                                       g_rep[:], init,
                                                       ALU.mult, ALU.add),
                                  deps=dep_sc)
                    e_cp_u[u] = p.step("pool", lambda e, par=par, u=u: e.tensor_copy(
                        Hb[:, u:u + 1], Htr[par][:, TH - 1:TH]), deps=[e_sc])
                    e_cp_par[par] = e_cp_u[u]
                    e_ym = p.step("dve", lambda e, par=par, dc=dc, h=h: (
                        e.tensor_mul(tmp_t[:], Htr[par][:], c_rep[:]),
                        e.tensor_add(Y[:, dc * T + h * TH: dc * T + (h + 1) * TH],
                                     Y[:, dc * T + h * TH: dc * T + (h + 1) * TH],
                                     tmp_t[:]))[-1],
                        deps=[e_sc])
                    e_ymul[par] = e_ym
                    e_scanB[par] = e_sc
                    ui += 1
        e_passB = [e_ymul[0], e_ymul[1]]

        if stage < 4:
            p.dma("pool", [(dbg["dbg_xmid"][dc * 128:(dc + 1) * 128, :],
                            Y[:, dc * T:(dc + 1) * T]) for dc in range(DCH)],
                  deps=e_passB)

    if full:
        free("E_t", "ss_all", "rcv_sb", "hin", "Sdt_sb", "Hb", "tmp_t",
             "c_rep", "g_rep", "Htr1", "Htr0", "barA1", "barA0")
        free("dt_all")
        Xf = alloc("X_ffn", [128, DCH * T], BF16, side="right")
        xb2 = alloc("xb2", [128, DCH * SLAB], BF16)
        sq2 = alloc("sq2", [128, DCH * SLAB], BF16)
        mu2_r = alloc("mu_r2", [1, SLAB], F32)
        vr2_r = alloc("vr_r2", [1, SLAB], F32)
        bw2_r = alloc("bw_r2", [1, SLAB], F32)
        arep2t = alloc("arep2", [128, SLAB], F32)
        brep2t = alloc("brep2", [128, SLAB], F32)
        zz = alloc("zz2", [128, SLAB], F32)
        ps2_su = palloc("ps_su2", [1, SLAB])
        ps2_sq = palloc("ps_sq2", [1, SLAB])

        e_ln2 = []
        e_prevbc = (None, 0)
        for s in range(NSL):
            c0 = s * SLAB
            e_cast = p.step("dve", lambda e, c0=c0: [
                e.tensor_copy(xb2[:, dc * SLAB:(dc + 1) * SLAB],
                              Y[:, dc * T + c0: dc * T + c0 + SLAB])
                for dc in range(DCH)][-1], deps=e_passB)
            e_sqr = p.step("act", lambda e, c0=c0: [
                e.activation(sq2[:, dc * SLAB:(dc + 1) * SLAB],
                             Y[:, dc * T + c0: dc * T + c0 + SLAB], AF.Square)
                for dc in range(DCH)][-1], deps=e_passB)
            e_st1 = p.step("pe", lambda e: [
                e.matmul(ps2_su[:], ones_b[:, 0:1], xb2[:, dc * SLAB:(dc + 1) * SLAB],
                         start=(dc == 0), stop=(dc == DCH - 1))
                for dc in range(DCH)][-1], deps=[e_cast, e_prevbc])
            e_st2 = p.step("pe", lambda e: [
                e.matmul(ps2_sq[:], ones_b[:, 0:1], sq2[:, dc * SLAB:(dc + 1) * SLAB],
                         start=(dc == 0), stop=(dc == DCH - 1))
                for dc in range(DCH)][-1], deps=[e_sqr, e_prevbc])
            e_mu = p.step("dve", lambda e: e.tensor_scalar_mul(mu2_r[:], ps2_su[:], 1.0 / D),
                          deps=[e_st1])
            e_vr = p.step("dve", lambda e: (
                e.tensor_mul(vr2_r[:], mu2_r[:], mu2_r[:]),
                e.scalar_tensor_tensor(vr2_r[:], ps2_sq[:], 1.0 / D, vr2_r[:],
                                       ALU.mult, ALU.subtract))[-1],
                deps=[e_mu, e_st2])
            e_rs = p.step("act", lambda e: (
                e.activation(vr2_r[:], vr2_r[:], AF.Ln, bias=cns[0:1, 0:1]),
                e.activation(vr2_r[:], vr2_r[:], AF.Exp, scale=cns[0:1, 1:2]))[-1], deps=[e_vr])
            e_bw = p.step("dve", lambda e: e.scalar_tensor_tensor(
                bw2_r[:], mu2_r[:], -1.0, vr2_r[:], ALU.mult, ALU.mult), deps=[e_rs])
            e_ro = p.dma("pool", [(lnrow_d[0:1, :], vr2_r[:]), (lnrow_d[1:2, :], bw2_r[:])],
                         deps=[e_bw])
            e_bc = p.dma("pool", [(arep2t[:], lnrow_d[0:1, :].broadcast_to([128, SLAB])),
                                  (brep2t[:], lnrow_d[1:2, :].broadcast_to([128, SLAB]))],
                         deps=[e_ro])
            e_prevbc = e_bc
            e_xn = p.step("dve", lambda e, c0=c0: [
                (e.tensor_mul(zz[:], Y[:, dc * T + c0: dc * T + c0 + SLAB], arep2t[:]),
                 e.tensor_add(Xf[:, dc * T + c0: dc * T + c0 + SLAB], zz[:], brep2t[:]))
                for dc in range(DCH)][-1][-1], deps=[e_bc, e_st1, e_st2])
            e_xm = p.dma("pool", [(xmid_d[dc * 128:(dc + 1) * 128, c0:c0 + SLAB],
                                   Y[:, dc * T + c0: dc * T + c0 + SLAB])
                                  for dc in range(DCH)], deps=[e_xn])
            e_ln2.append((e_xn, e_xm))

        free("zz2", "brep2", "arep2", "bw_r2", "vr_r2", "mu_r2", "sq2", "xb2")
        free("Y_all")
        h1b = alloc("h1b", [128, MH * TH], BF16)
        h2b = alloc("h2b", [128, MH * TH], BF16)
        st_t = [alloc("st_t0", [128, SLAB], BF16), alloc("st_t1", [128, SLAB], BF16)]
        t2_t = [alloc("t2_t0", [128, SLAB], BF16), alloc("t2_t1", [128, SLAB], BF16)]
        ob_t = [alloc("ob_t0", [128, SLAB], F32), alloc("ob_t1", [128, SLAB], F32)]
        xm_t = [alloc("xm_t0", [128, SLAB], F32), alloc("xm_t1", [128, SLAB], F32)]
        ps_fa = [palloc("ps_fa0", [128, SLAB]), palloc("ps_fa1", [128, SLAB])]
        ps_fb = [palloc("ps_fb0", [128, SLAB]), palloc("ps_fb1", [128, SLAB])]

        e_lnall = [e for pair in e_ln2 for e in pair]
        e_xn_ready = [e_ln2[s][0] for s in range(NSL)]
        e_xm_ready = [e_ln2[s][1] for s in range(NSL)]

        e_wdma = {}
        e_peA = {}
        e_peB = {}
        e_evA = {}
        e_evB = {}
        e_out_total = None

        ln2_all = e_xn_ready + e_xm_ready
        for H in range(2):
            xn_dep = list(ln2_all)
            for m in range(MH):
                par = m % 2
                e_w = p.dma("sp", [
                    (wcur[par][:, 0:1024], w1m_h[m].rearrange("p k c -> p (k c)")),
                    (wcur[par][:, 1024:2048], w2m_h[m].rearrange("p k c -> p (k c)"))],
                    deps=[e_peA.get(("prev", par), (None, 0))])
                e_wdma[par] = e_w
                for sl in range(2):
                    t0 = H * TH + sl * SLAB
                    pp = (m * 2 + sl) % 2
                    e_mA = p.step("pe", lambda e, par=par, t0=t0, pp=pp: [
                        e.matmul(ps_fa[pp][:], wcur[par][:, k * 128:(k + 1) * 128],
                                 Xf[:, k * T + t0: k * T + t0 + SLAB],
                                 start=(k == 0), stop=(k == DCH - 1))
                        for k in range(DCH)][-1],
                        deps=[e_w, e_evA.get(pp, (None, 0))] + xn_dep)
                    e_mB = p.step("pe", lambda e, par=par, t0=t0, pp=pp: [
                        e.matmul(ps_fb[pp][:], wcur[par][:, 1024 + k * 128:1024 + (k + 1) * 128],
                                 Xf[:, k * T + t0: k * T + t0 + SLAB],
                                 start=(k == 0), stop=(k == DCH - 1))
                        for k in range(DCH)][-1],
                        deps=[e_w, e_evB.get(pp, (None, 0))])
                    e_s = p.step("act", lambda e, m=m, pp=pp: (
                        e.activation(st_t[pp][:], ps_fa[pp][:], AF.Silu,
                                     scale=s1_sc[:, m:m + 1]),
                        e.activation(t2_t[pp][:], ps_fb[pp][:], AF.Copy,
                                     scale=s2_sc[:, m:m + 1]))[-1],
                        deps=[e_mA, e_mB])
                    e_h = p.step("dve", lambda e, m=m, sl=sl, pp=pp: e.tensor_mul(
                        h1b[:, m * TH + sl * SLAB: m * TH + (sl + 1) * SLAB],
                        st_t[pp][:], t2_t[pp][:]), deps=[e_s])
                    e_evA[pp] = e_h
                    e_evB[pp] = e_h
                e_peA[(H, "L1", par)] = e_mA
                e_peB[(H, "L1", par)] = e_mB
                e_peA[("prev", par)] = e_mB
            l1_done = [e_evA.get(0, (None, 0)), e_evA.get(1, (None, 0))]
            for layer, (wm_h, sc_sb, src, dst) in enumerate([
                    (h1m_h, sh1_sc, h1b, h2b), (h2m_h, sh2_sc, h2b, h1b)]):
                prev_dep = l1_done if layer == 0 else [e_evA[0], e_evA[1]]
                for m in range(MH):
                    par = m % 2
                    e_w = p.dma("sp", [(wcur[par][:], wm_h[m].rearrange("p k c -> p (k c)"))],
                                deps=[e_peA.get(("prev", par), (None, 0))])
                    for sl in range(2):
                        sl0 = sl * SLAB
                        pp = (m * 2 + sl) % 2
                        e_mm = p.step("pe", lambda e, par=par, sl0=sl0, pp=pp, src=src: [
                            e.matmul(ps_fa[pp][:], wcur[par][:, k * 128:(k + 1) * 128],
                                     src[:, k * TH + sl0: k * TH + sl0 + SLAB],
                                     start=(k == 0), stop=(k == MH - 1))
                            for k in range(MH)][-1],
                            deps=[e_w, e_evA.get(pp, (None, 0))] + prev_dep)
                        e_g2 = p.step("act", lambda e, m=m, sl0=sl0, pp=pp, dst=dst, sc_sb=sc_sb:
                                      e.activation(dst[:, m * TH + sl0: m * TH + sl0 + SLAB],
                                                   ps_fa[pp][:], AF.Gelu,
                                                   scale=sc_sb[:, m:m + 1]),
                                      deps=[e_mm])
                        e_evA[pp] = e_g2
                    e_peA[("prev", par)] = e_mm
            for m in range(DCH):
                par = m % 2
                e_w = p.dma("sp", [(wcur[par][:], w3m_h[m].rearrange("p k c -> p (k c)"))],
                            deps=[e_peA.get(("prev", par), (None, 0))])
                for sl in range(2):
                    sl0 = sl * SLAB
                    t0 = H * TH + sl0
                    pp = (m * 2 + sl) % 2
                    e_mm = p.step("pe", lambda e, par=par, sl0=sl0, pp=pp: [
                        e.matmul(ps_fa[pp][:], wcur[par][:, k * 128:(k + 1) * 128],
                                 h1b[:, k * TH + sl0: k * TH + sl0 + SLAB],
                                 start=(k == 0), stop=(k == MH - 1))
                        for k in range(MH)][-1],
                        deps=[e_w, e_evA.get(pp, (None, 0)), e_evB.get((9, pp), (None, 0))])
                    e_o = p.step("act", lambda e, m=m, pp=pp: e.activation(
                        ob_t[pp][:], ps_fa[pp][:], AF.Copy, scale=s3_sc[:, m:m + 1]),
                        deps=[e_mm, e_evB.get((9, pp), (None, 0))])
                    e_evA[pp] = e_o
                    e_xm = p.dma("pool", [(xm_t[pp][:], xmid_d[m * 128:(m + 1) * 128,
                                                               t0:t0 + SLAB])],
                                 deps=[e_xm_ready[2 * H + sl], e_o,
                                       e_evB.get((8, pp), (None, 0))])
                    e_add = p.step("dve", lambda e, pp=pp: e.tensor_add(
                        ob_t[pp][:], ob_t[pp][:], xm_t[pp][:]), deps=[e_o, e_xm])
                    e_evB[(8, pp)] = e_add
                    e_od = p.dma("pool", [(out_h[m * 128:(m + 1) * 128, t0:t0 + SLAB],
                                           ob_t[pp][:])], deps=[e_add])
                    e_evB[(9, pp)] = e_od
                    e_out_total = e_od
                e_peA[("prev", par)] = e_mm

        if stage == 5:
            p.dma("pool", [(dbg["dbg_xmid"][:], xmid_d[:])],
                  deps=[e for pr in e_ln2 for e in pr])
        assert e_out_total is not None
        p.wait_only("pool", [(s, p.cnt[s]) for s in p.dma_pools["pool"]])
    elif stage >= 1:
        p.wait_only("pool", [(s, p.cnt[s]) for s in p.dma_pools["pool"]])

    with nc.Block() as block:
        p.emit(block)

    return nc


def _shard_inputs(inputs):
    x = np.ascontiguousarray(inputs["L0_attn_x"])
    bf = ml_dtypes.bfloat16

    def tile_w(w, kk, mm):
        o, i = w.shape
        r = w.reshape(mm, 128, kk, 128)
        return np.ascontiguousarray(r.transpose(0, 3, 2, 1)).astype(bf)

    w1m = tile_w(np.asarray(inputs["L0_ffn_w1_w"], np.float32), 8, 32)
    w2m = tile_w(np.asarray(inputs["L0_ffn_w2_w"], np.float32), 8, 32)
    h1m = tile_w(np.asarray(inputs["L0_ffn_h1_w"], np.float32), 32, 32)
    h2m = tile_w(np.asarray(inputs["L0_ffn_h2_w"], np.float32), 32, 32)
    w3m = tile_w(np.asarray(inputs["L0_ffn_w3_w"], np.float32), 32, 8)

    def cols(v, k):
        return np.ascontiguousarray(np.asarray(v, np.float32).reshape(k, 128).T)

    common = dict(
        w_in=np.ascontiguousarray(np.asarray(inputs["L0_attn_w_in"], np.float32)),
        w_x=np.ascontiguousarray(np.asarray(inputs["L0_attn_w_x"], np.float32)),
        w_dt=np.ascontiguousarray(np.asarray(inputs["L0_attn_w_dt"], np.float32)),
        b_dt=cols(inputs["L0_attn_b_dt"], 8),
        A_log=np.ascontiguousarray(np.asarray(inputs["L0_attn_A_log"], np.float32)),
        Dp=cols(inputs["L0_attn_Dp"], 8),
        w1m=w1m, w2m=w2m, h1m=h1m, h2m=h2m, w3m=w3m,
        s1=cols(inputs["L0_ffn_w1_s"], 32),
        s2=cols(inputs["L0_ffn_w2_s"], 32),
        sh1=cols(inputs["L0_ffn_h1_s"], 32),
        sh2=cols(inputs["L0_ffn_h2_s"], 32),
        s3=cols(inputs["L0_ffn_w3_s"], 8),
        onesc=np.ones((128, 1), np.float32),
        cns=np.ascontiguousarray(np.broadcast_to(
            np.array([1e-5, -0.5, -1.0, 1.0], np.float32), (128, 4))),
    )

    in_maps = []
    for c in range(NCORES):
        bat, pos = c // 4, c % 4
        xs = np.ascontiguousarray(
            np.asarray(x[bat, pos * T:(pos + 1) * T, :], np.float32).T)
        m2b = np.zeros((8, 9), np.float32)
        for j in range(8):
            incl = (j // 4 == bat) and (j % 4 < pos)
            m2b[j, 8] = 0.0 if incl else BIG
            for i in range(8):
                m2b[j, i] = 1.0 if (incl and i // 4 == bat
                                    and j % 4 < i % 4 < pos) else 0.0
        m2b_t = np.ascontiguousarray(
            np.broadcast_to(m2b.reshape(1, 72), (128, 72)))
        im = dict(common)
        im["x"] = xs
        im["m2b"] = m2b_t
        in_maps.append(im)
    return in_maps


def kernel(**inputs):
    in_maps = _shard_inputs(inputs)
    nc = build_program(4)
    res = run_bass_kernel_spmd(nc, in_maps, core_ids=list(range(NCORES)))
    out = np.empty((2, 8192, 1024), np.float32)
    for c in range(NCORES):
        bat, pos = c // 4, c % 4
        out[bat, pos * T:(pos + 1) * T, :] = res.results[c]["out"].T
    return out
